# version 43
# baseline (speedup 1.0000x reference)
import contextlib

import numpy as np
import ml_dtypes

import concourse.bass as bass
import concourse.mybir as mybir
from concourse.bass_utils import run_bass_kernel_spmd

F32 = mybir.dt.float32
BF16 = mybir.dt.bfloat16
AF = mybir.ActivationFunctionType
ALU = mybir.AluOpType

N_CORES = 8
N = 2048
D = 1024
T = 16
EPS = 1e-5
INV_SQRT_DK = 1.0 / np.sqrt(128.0)


def build_nc():
    nc = bass.Bass("TRN2", target_bir_lowering=False,
                   detect_race_conditions=False)

    x_d = nc.dram_tensor("x", [N, D], F32, kind="ExternalInput").ap()
    wqt_d = nc.dram_tensor("wqt", [128, 1024], F32, kind="ExternalInput").ap()
    wkt_d = nc.dram_tensor("wkt", [128, 1024], F32, kind="ExternalInput").ap()
    gam_d = nc.dram_tensor("gam", [128, 1024], F32, kind="ExternalInput").ap()
    bet_d = nc.dram_tensor("bet", [128, 1024], F32, kind="ExternalInput").ap()
    win_d = nc.dram_tensor("win", [128, 512], F32, kind="ExternalInput").ap()
    bta_d = nc.dram_tensor("bta", [65, 1024], F32, kind="ExternalInput").ap()
    cta_d = nc.dram_tensor("cta", [128, 512], F32, kind="ExternalInput").ap()
    woa_d = nc.dram_tensor("woa", [65, 1024], F32, kind="ExternalInput").ap()
    b2c_d = nc.dram_tensor("b2c", [1, 64], F32, kind="ExternalInput").ap()
    binc_d = nc.dram_tensor("binc", [1, 64], F32, kind="ExternalInput").ap()
    idb_d = nc.dram_tensor("idb", [128, 128], BF16, kind="ExternalInput").ap()
    y_d = nc.dram_tensor("y", [N, D], F32, kind="ExternalOutput").ap()

    with contextlib.ExitStack() as ctx:
        ent = ctx.enter_context
        X = ent(nc.sbuf_tensor("X", [128, T * D], F32))
        XNB = ent(nc.sbuf_tensor("XNB", [128, T * D], BF16))
        Z2B = ent(nc.sbuf_tensor("Z2B", [128, T * D], BF16))
        XT = ent(nc.sbuf_tensor("XT", [128, 8 * 2048], BF16,
                                is_transpose_dest=True))
        WB = ent(nc.sbuf_tensor("WB", [128, 2 * D], BF16))
        KB = ent(nc.sbuf_tensor("KB", [128, 2048], BF16))
        QTB = ent(nc.sbuf_tensor("QTB", [128, 2048], BF16))
        MB = ent(nc.sbuf_tensor("MB", [128, 1024], BF16))
        SST = ent(nc.sbuf_tensor("SST", [65, 2048], BF16))
        GT = ent(nc.sbuf_tensor("GT", [65, 2048], BF16))
        WQTB = ent(nc.sbuf_tensor("WQTB", [128, 1024], BF16))
        WKTB = ent(nc.sbuf_tensor("WKTB", [128, 1024], BF16))
        GAMB = ent(nc.sbuf_tensor("GAMB", [128, 1024], BF16))
        BETB = ent(nc.sbuf_tensor("BETB", [128, 1024], BF16))
        WINB = ent(nc.sbuf_tensor("WINB", [128, 512], BF16))
        BTAB = ent(nc.sbuf_tensor("BTAB", [65, 1024], BF16))
        CTAB = ent(nc.sbuf_tensor("CTAB", [128, 512], BF16))
        WOAB = ent(nc.sbuf_tensor("WOAB", [65, 1024], BF16))
        B2CB = ent(nc.sbuf_tensor("B2CB", [1, 64], BF16))
        BINCB = ent(nc.sbuf_tensor("BINCB", [1, 64], BF16))
        IDB = ent(nc.sbuf_tensor("IDB", [128, 128], BF16))
        IDF32 = ent(nc.sbuf_tensor("IDF32", [128, 128], F32))
        ONESB = ent(nc.sbuf_tensor("ONESB", [1, 512], BF16))
        STG0 = ent(nc.sbuf_tensor("STG0", [128, 1024], F32))
        STG1 = ent(nc.sbuf_tensor("STG1", [128, 1024], F32))
        ST = ent(nc.sbuf_tensor("ST", [128, 136], F32))
        MPS = ent(nc.psum_tensor("MPS", [128, 1024], F32))
        KQ0 = ent(nc.psum_tensor("KQ0", [128, 512], F32))
        KQ1 = ent(nc.psum_tensor("KQ1", [128, 512], F32))
        JP0 = ent(nc.psum_tensor("JP0", [128, 1024], F32))
        JP1 = ent(nc.psum_tensor("JP1", [128, 1024], F32))
        sem = lambda name: ent(nc.semaphore(name))
        s_x = [sem(f"s_xin{k}") for k in range(8)]
        s_wl = [sem("s_wa"), sem("s_wb")]
        s_idb = sem("s_idb")
        s_wc = sem("s_wc")
        s_nb = sem("s_nb")
        s_trs = [sem("s_tr0"), sem("s_tr1")]
        s_vs = sem("s_vs")
        s_sq = sem("s_sq")
        s_kq = sem("s_kq")
        s_ev = sem("s_ev")
        s_m = sem("s_m")
        s_mev = sem("s_mev")
        s_j = sem("s_j")
        s_z2 = sem("s_z2")
        s_wb = sem("s_wbt")
        s_x1 = sem("s_x1")
        s_smm = sem("s_smm")
        s_sev = sem("s_sev")
        s_hmm = sem("s_hmm")
        s_hev = sem("s_hev")
        s_hbs = sem("s_hbs")
        s_hbv = sem("s_hbv")
        s_gmm = sem("s_gmm")
        s_gev = sem("s_gev")
        s_y = sem("s_y")
        s_yv = sem("s_yv")
        s_out = sem("s_out")
        s_idf = sem("s_idf")
        blk = ent(nc.Block())
        JP = [JP0, JP1]
        HROT = [JP0, JP1, MPS]
        KQ = [KQ0, KQ1]

        NS1, NS2, NS3 = 6, 4, 4
        SLOT_LN1 = lambda s: 24 + 8 * (s % NS1)
        SLOT_LN2 = lambda t: 72 + 8 * (t % NS2)
        SLOT_LN3 = lambda t: 104 + 8 * (t % NS3)

        fin_order = []
        for i in range(T + 3):
            if 2 <= i <= T + 1:
                fin_order.append(("L1", i - 2))
        for t in range(T + 4):
            if t < T:
                fin_order.append(("L2", t))
            if 3 <= t <= T + 2:
                fin_order.append(("L3", t - 3))
        ses_val = {k: idx + 1 for idx, k in enumerate(fin_order)}

        wplan = [
            (wkt_d, WKTB, 128, 1024),
            (wqt_d, WQTB, 128, 1024),
            (gam_d, GAMB, 128, 1024),
            (bet_d, BETB, 128, 1024),
            (win_d, WINB, 128, 512),
            (bta_d, BTAB, 65, 1024),
            (cta_d, CTAB, 128, 512),
            (woa_d, WOAB, 65, 1024),
            (b2c_d, B2CB, 1, 64),
            (binc_d, BINCB, 1, 64),
        ]
        NW = len(wplan)
        xs_after = {1: [2, 3], 2: [4, 5], 3: [6, 7]}

        @blk.gpsimd
        def _(g):
            g.dma_start(X[:, 0:D], x_d[0:128, :]).then_inc(s_x[0], 16)
            g.dma_start(X[:, D:2 * D], x_d[128:256, :]).then_inc(s_x[0], 16)
            g.dma_start(IDB[:, :], idb_d).then_inc(s_idb, 16)
            for i, (dsrc, dst, rows, cols) in enumerate(wplan):
                stg = [STG0, STG1][i % 2]
                if i >= 2:
                    g.wait_ge(s_wc, i - 1)
                g.dma_start(stg[0:rows, 0:cols], dsrc).then_inc(s_wl[i % 2], 16)
                for t in xs_after.get(i, []):
                    g.dma_start(X[:, t * D:(t + 1) * D],
                                x_d[t * 128:(t + 1) * 128, :]
                                ).then_inc(s_x[t // 2], 16)
            for t in range(T):
                g.wait_ge(s_yv, 2 * (t + 1))
                g.dma_start(y_d[t * 128:(t + 1) * 128, :],
                            X[:, t * D:(t + 1) * D]).then_inc(s_out, 16)
            g.wait_ge(s_out, 16 * T)

        def tr_wait_vals(u_max):
            return [16 * ((u_max + 2 - p) // 2) for p in range(2)]

        @blk.sync
        def _(sy):
            tot = [0, 0]
            for u in range(2 * T):
                if u < 8:
                    t8 = 8 + u
                    sy.dma_start(X[:, t8 * D:(t8 + 1) * D],
                                 x_d[t8 * 128:(t8 + 1) * 128, :]
                                 ).then_inc(s_x[t8 // 2], 16)
                p = u % 2
                if u >= 2:
                    sy.wait_ge(s_trs[p], tot[p])
                sy.wait_ge(s_nb, u + 1)
                t = u % T
                srcb = XNB if u < T else Z2B
                out3 = XT[:, 0:8 * 2048].rearrange(
                    "p (j n) -> p j n", n=2048)[:, :, t * 128:(t + 1) * 128]
                sy.dma_start_transpose(
                    out3, srcb[:, t * D:(t + 1) * D]).then_inc(s_trs[p], 16)
                tot[p] += 16

        @blk.vector
        def _(v):
            pos = [0]
            lastw = {}
            GAP = 3

            def V(fn, reads=(), writes=()):
                need = max((lastw.get(k, -99) + 1 + GAP for k in reads),
                           default=0)
                while pos[0] < need:
                    v.engine_nop()
                    pos[0] += 1
                r = fn()
                pos[0] += 1
                for k in writes:
                    lastw[k] = pos[0] - 1
                return r

            def stats_ab(scr, h0, h1):
                V(lambda: v.bn_stats(ST[:, scr:scr + 6], h0),
                  writes=(("scr", scr),))
                V(lambda: v.bn_stats(ST[:, scr + 6:scr + 12], h1),
                  writes=(("scr", scr),))

            def stats_g(slot, scr, gate=None, gate2=None):
                if gate is not None:
                    v.wait_ge(s_sq, gate)
                if gate2 is not None:
                    v.wait_ge(*gate2)
                V(lambda: v.bn_aggr(
                    ST[:, slot:slot + 2],
                    ST[:, scr:scr + 12].rearrange("p (g s) -> p g s", g=2)),
                  reads=(("scr", scr),), writes=((slot, "mv"),))

            def stats_nvr(slot):
                V(lambda: v.tensor_scalar_mul(ST[:, slot + 4:slot + 5],
                                              ST[:, slot:slot + 1], -1.0),
                  reads=((slot, "mv"),), writes=((slot, "nm"),))
                V(lambda: v.tensor_scalar_add(ST[:, slot + 2:slot + 3],
                                              ST[:, slot + 1:slot + 2], EPS),
                  reads=((slot, "mv"),), writes=((slot, "vpe"),))
                V(lambda: v.reciprocal(ST[:, slot + 3:slot + 4],
                                       ST[:, slot + 2:slot + 3]
                                       ).then_inc(s_vs, 1),
                  reads=((slot, "vpe"),), writes=((slot, "inv"),))

            def ln_apply(ses, out_ap, in_ap, sem_):
                kind, u = ses
                slot = {"L1": SLOT_LN1, "L2": SLOT_LN2,
                        "L3": SLOT_LN3}[kind](u)
                v.wait_ge(s_sq, ses_val[ses])
                return V(lambda: v.scalar_tensor_tensor(
                    out_ap, in_ap, ST[:, slot + 4:slot + 5],
                    ST[:, slot + 5:slot + 6].broadcast_to((128, 1024)),
                    op0=ALU.add, op1=ALU.mult).then_inc(sem_, 1),
                    reads=((slot, "nm"),))

            V(lambda: v.memset(ONESB[:, :], 1.0))
            V(lambda: v.memset(SST[64:65, :], 1.0))
            V(lambda: v.memset(GT[64:65, :], 1.0))

            for i in range(T + 2):
                if i < T:
                    v.wait_ge(s_x[i // 2], 32)
                    stats_ab(12 * (i % 2), X[:, i * D: i * D + 512],
                             X[:, i * D + 512: i * D + 1024])
                if 1 <= i <= T:
                    s = i - 1
                    stats_g(SLOT_LN1(s), 12 * (s % 2),
                            gate=(ses_val[("L1", s - NS1)]
                                  if s >= NS1 else None))
                if 2 <= i <= T + 1:
                    stats_nvr(SLOT_LN1(i - 2))
                if 4 <= i <= T:
                    t = i - 4
                    ln_apply(("L1", t), XNB[:, t * D:(t + 1) * D],
                             X[:, t * D:(t + 1) * D], s_nb)
                if i == T:
                    for t in (T - 3, T - 2):
                        ln_apply(("L1", t), XNB[:, t * D:(t + 1) * D],
                                 X[:, t * D:(t + 1) * D], s_nb)
                if i == T + 1:
                    t = T - 1
                    ln_apply(("L1", t), XNB[:, t * D:(t + 1) * D],
                             X[:, t * D:(t + 1) * D], s_nb)

            JROT = [MPS, JP0, JP1]
            for t in range(T + 4):
                if t < T:
                    p = JROT[t % 3]
                    v.wait_ge(s_j, t + 1)
                    stats_ab(0, p[:, 0:512], p[:, 512:1024])
                if 3 <= t <= T + 2:
                    u3 = t - 3
                    stats_ab(12, X[:, u3 * D: u3 * D + 512],
                             X[:, u3 * D + 512: u3 * D + 1024])
                if t < T:
                    stats_g(SLOT_LN2(t), 0,
                            gate2=((s_z2, t - 3) if t >= NS2 else None))
                if 3 <= t <= T + 2:
                    stats_g(SLOT_LN3(t - 3), 12,
                            gate=(ses_val[("L3", t - 3 - NS3)]
                                  if t - 3 >= NS3 else None))
                if t < T:
                    stats_nvr(SLOT_LN2(t))
                if 3 <= t <= T + 2:
                    stats_nvr(SLOT_LN3(t - 3))
                if 2 <= t <= T + 1:
                    u = t - 2
                    v.wait_ge(s_z2, u + 1)
                    wsl = WB[:, (u % 2) * D:(u % 2 + 1) * D]
                    V(lambda: v.tensor_tensor(wsl, Z2B[:, u * D:(u + 1) * D],
                                              GAMB[:, :], op=ALU.mult))
                    V(lambda: v.tensor_tensor(wsl, wsl, BETB[:, :],
                                              op=ALU.add))
                    V(lambda: v.tensor_tensor(X[:, u * D:(u + 1) * D],
                                              X[:, u * D:(u + 1) * D], wsl,
                                              op=ALU.add).then_inc(s_x1, 1))

            def relu_b(c):
                jp = HROT[c % 3]
                v.wait_ge(s_hmm, c + 1)
                last = None
                for j in range(4, 8):
                    last = v.tensor_scalar_max(
                        XNB[:, j * 2048 + c * 128: j * 2048 + (c + 1) * 128],
                        jp[:, j * 128:(j + 1) * 128], 0.0)
                last.then_inc(s_hbv, 1)

            for c in range(4, 16):
                relu_b(c)

        @blk.scalar
        def _(s):
            s.preload_act_table(AF.Sqrt)
            SLOT = {"L1": SLOT_LN1, "L2": SLOT_LN2, "L3": SLOT_LN3}

            def emit_cast(i):
                dsrc, dst, rows, cols = wplan[i]
                stg = [STG0, STG1][i % 2]
                s.wait_ge(s_wl[i % 2], 16 * (i // 2 + 1))
                s.activation(dst[0:rows, 0:cols], stg[0:rows, 0:cols],
                             AF.Copy).then_inc(s_wc, 1)

            def sev(cc):
                s.wait_ge(s_smm, cc + 1)
                s.activation(SST[0:64, cc * 512:(cc + 1) * 512],
                             KQ[cc % 2][0:64, :], AF.Copy).then_inc(s_sev, 1)

            emit_cast(0)
            emit_cast(1)
            s.wait_ge(s_idb, 16)
            s.activation(IDF32[:, :], IDB[:, :], AF.Copy).then_inc(s_idf, 1)

            def kq_evac(t):
                s.wait_ge(s_kq, t + 1)
                kq = KQ[t % 2]
                s.activation(KB[:, t * 128:(t + 1) * 128], kq[:, 0:128],
                             AF.Copy)
                s.activation(QTB[:, t * 128:(t + 1) * 128], kq[:, 128:256],
                             AF.Copy, scale=INV_SQRT_DK).then_inc(s_ev, 1)

            def relu_a(c, full=False):
                jp = HROT[c % 3]
                s.wait_ge(s_hmm, c + 1)
                last = None
                for j in range(0, 4):
                    last = s.activation(
                        XNB[:, j * 2048 + c * 128: j * 2048 + (c + 1) * 128],
                        jp[:, j * 128:(j + 1) * 128], AF.Relu)
                last.then_inc(s_hev, 1)
                if full:
                    for j in range(4, 8):
                        last = s.activation(
                            XNB[:, j * 2048 + c * 128: j * 2048 + (c + 1) * 128],
                            jp[:, j * 128:(j + 1) * 128], AF.Relu)
                    last.then_inc(s_hbs, 1)

            pend_nr = []
            pend_ap = []
            JROT_S = [MPS, JP0, JP1]

            def ln_flush(pad=0):
                if pend_ap:
                    kind, u = pend_ap.pop(0)
                    slot = SLOT[kind](u)
                    for _ in range(pad):
                        s.nop()
                    if kind == "L3":
                        s.activation(Z2B[:, u * D:(u + 1) * D],
                                     X[:, u * D:(u + 1) * D], AF.Identity,
                                     bias=ST[:, slot + 6:slot + 7],
                                     scale=ST[:, slot + 5:slot + 6]
                                     ).then_inc(s_nb, 1)
                    else:
                        s.activation(Z2B[:, u * D:(u + 1) * D],
                                     JROT_S[u % 3][:, :], AF.Identity,
                                     bias=ST[:, slot + 6:slot + 7],
                                     scale=ST[:, slot + 5:slot + 6]
                                     ).then_inc(s_z2, 1)
                if pend_nr:
                    kind, u = pend_nr.pop(0)
                    slot = SLOT[kind](u)
                    s.activation(ST[:, slot + 6:slot + 7],
                                 ST[:, slot + 4:slot + 5], AF.Identity,
                                 bias=0.0, scale=ST[:, slot + 5:slot + 6])
                    pend_ap.append((kind, u))

            for ses in fin_order:
                kind, idx = ses
                slot = SLOT[kind](idx)
                s.wait_ge(s_vs, ses_val[ses])
                s.activation(ST[:, slot + 5:slot + 6],
                             ST[:, slot + 3:slot + 4],
                             AF.Sqrt).then_inc(s_sq, 1)
                ln_flush()
                if kind == "L1" and idx < 8:
                    emit_cast(idx + 2)
                if kind != "L1":
                    pend_nr.append((kind, idx))
                if kind == "L1" and idx >= 3:
                    kq_evac(idx - 3)
                if ses == ("L1", T - 1):
                    for tt in (T - 3, T - 2, T - 1):
                        kq_evac(tt)
                    s.wait_ge(s_m, 1)
                    s.activation(MB[:, 0:512], MPS[:, 0:512], AF.Copy)
                    s.activation(MB[:, 512:1024], MPS[:, 512:1024],
                                 AF.Copy).then_inc(s_mev, 1)
                if ses == ("L3", 12):
                    sev(0)
                    sev(1)
                if ses == ("L3", 13):
                    relu_a(0, full=True)
                if ses == ("L3", 14):
                    relu_a(1, full=True)
                if ses == ("L3", 15):
                    sev(2)
            while pend_nr or pend_ap:
                ln_flush(pad=4)

            def gev(cc):
                s.wait_ge(s_gmm, cc + 1)
                s.activation(GT[0:64, cc * 512:(cc + 1) * 512],
                             KQ[cc % 2][0:64, :],
                             AF.Copy).then_inc(s_gev, 1)

            relu_a(2, full=True)
            relu_a(3, full=True)
            relu_a(4)
            relu_a(5)
            relu_a(6)
            relu_a(7)
            sev(3)
            gev(0)
            for c in range(8, 12):
                relu_a(c)
            gev(1)
            def y_copy(t):
                kqy = t >= 8 and t % 2 == 1
                for h in range(2):
                    s.wait_ge(s_y, 2 * t + h + 1)
                    src_ = (KQ0[:, :] if h == 0 else KQ1[:, :]) if kqy \
                        else MPS[:, h * 512:(h + 1) * 512]
                    s.activation(X[:, t * D + h * 512: t * D + (h + 1) * 512],
                                 src_, AF.Copy).then_inc(s_yv, 1)

            for c in range(12, 16):
                relu_a(c)
            for t in range(4):
                y_copy(t)
            gev(2)
            for t in range(4, 8):
                y_copy(t)
            gev(3)
            for t in range(8, 16):
                y_copy(t)

        @blk.tensor
        def _(te):
            te.wait_ge(s_wc, 2)
            te.wait_ge(s_idb, 16)

            def xt_t(j, t):
                return XT[:, j * 2048 + t * 128: j * 2048 + (t + 1) * 128]

            for t in range(T):
                te.wait_ge(s_trs[t % 2], 16 * (t // 2 + 1))
                if t >= 2:
                    te.wait_ge(s_ev, t - 1)
                kq = KQ[t % 2]
                for j in range(8):
                    te.matmul(kq[:, 0:128], lhsT=xt_t(j, t),
                              rhs=WKTB[:, j * 128:(j + 1) * 128],
                              start=(j == 0), stop=False)
                last = None
                for j in range(8):
                    last = te.matmul(kq[:, 128:256],
                                     lhsT=WQTB[:, j * 128:(j + 1) * 128],
                                     rhs=xt_t(j, t),
                                     start=False, stop=(j == 7))
                last.then_inc(s_kq, 1)
                te.wait_ge(s_ev, t + 1)
                for h in range(2):
                    last = te.matmul(
                        MPS[:, h * 512:(h + 1) * 512],
                        lhsT=KB[:, t * 128:(t + 1) * 128],
                        rhs=XNB[:, t * D + h * 512: t * D + (h + 1) * 512],
                        start=(t == 0), stop=(t == 15))
            last.then_inc(s_m, 1)

            def mm_s(cc):
                wv = tr_wait_vals(16 + 4 * cc + 3)
                te.wait_ge(s_trs[0], wv[0])
                te.wait_ge(s_trs[1], wv[1])
                if cc == 2:
                    te.wait_ge(s_sev, 1)
                elif cc == 3:
                    te.wait_ge(s_sev, 2)
                sp = KQ[cc % 2]
                for j in range(8):
                    te.matmul(sp[0:64, 0:512],
                              lhsT=WINB[:, j * 64:(j + 1) * 64],
                              rhs=XT[:, j * 2048 + cc * 512: j * 2048 + (cc + 1) * 512],
                              start=(j == 0), stop=False)
                te.matmul(sp[0:64, 0:512], lhsT=BINCB[:, :],
                          rhs=ONESB[:, :],
                          start=False, stop=True).then_inc(s_smm, 1)

            te.wait_ge(s_mev, 1)
            te.wait_ge(s_wc, NW)
            te.wait_ge(s_ev, T)
            JROT = [MPS, JP0, JP1]
            for t in range(T):
                if t >= 3:
                    te.wait_ge(s_z2, t - 2)
                p = JROT[t % 3]
                for h in range(2):
                    te.matmul(p[:, h * 512:(h + 1) * 512],
                              lhsT=QTB[:, t * 128:(t + 1) * 128],
                              rhs=MB[:, h * 512:(h + 1) * 512],
                              start=True, stop=False)
                    last = te.matmul(
                        p[:, h * 512:(h + 1) * 512],
                        lhsT=IDB[:, :],
                        rhs=XNB[:, t * D + h * 512: t * D + (h + 1) * 512],
                        start=False, stop=True)
                last.then_inc(s_j, 1)
            mm_s(0)
            mm_s(1)

            def mm_h(c):
                te.wait_ge(s_sev, c // 4 + 1)
                if c == 0:
                    te.wait_ge(s_z2, 14)
                elif c == 1:
                    te.wait_ge(s_z2, 15)
                elif c == 2:
                    te.wait_ge(s_z2, T)
                else:
                    cp = c - 3
                    te.wait_ge(s_hev, cp + 1)
                    te.wait_ge(s_hbs, min(cp + 1, 4))
                    if cp >= 4:
                        te.wait_ge(s_hbv, cp - 3)
                hp = HROT[c % 3]
                last = None
                for j in range(8):
                    last = te.matmul(hp[:, j * 128:(j + 1) * 128],
                                     lhsT=BTAB[0:65, j * 128:(j + 1) * 128],
                                     rhs=SST[0:65, c * 128:(c + 1) * 128],
                                     start=True, stop=True)
                last.then_inc(s_hmm, 1)

            def mm_g(cc):
                te.wait_ge(s_hev, 4 * cc + 4)
                te.wait_ge(s_hbs, 4)
                if cc >= 1:
                    te.wait_ge(s_hbv, 4 * cc)
                if cc < 2:
                    te.wait_ge(s_sev, cc + 3)
                gv = KQ[cc % 2][0:64, :]
                for j in range(8):
                    te.matmul(gv, lhsT=CTAB[:, j * 64:(j + 1) * 64],
                              rhs=XNB[:, j * 2048 + cc * 512: j * 2048 + (cc + 1) * 512],
                              start=(j == 0), stop=False)
                te.matmul(gv, lhsT=B2CB[:, :],
                          rhs=ONESB[:, :],
                          start=False, stop=True).then_inc(s_gmm, 1)

            def mm_y(cc, k):
                t = 4 * cc + k
                kqy = t >= 8 and t % 2 == 1
                if t == 0:
                    te.wait_ge(s_hev, 15)
                    te.wait_ge(s_hbv, 11)
                    te.wait_ge(s_idf, 1)
                te.wait_ge(s_x1, t + 1)
                te.wait_ge(s_gev, 4 if kqy else cc + 1)
                for h in range(2):
                    if t >= 10:
                        te.wait_ge(s_yv, 2 * (t - 2) + h + 1)
                    elif t == 8 or (1 <= t < 8):
                        te.wait_ge(s_yv, 2 * (t - 1) + h + 1)
                    bank = (KQ0[:, :] if h == 0 else KQ1[:, :]) if kqy \
                        else MPS[:, h * 512:(h + 1) * 512]
                    te.matmul(bank,
                              lhsT=GT[0:65, t * 128:(t + 1) * 128],
                              rhs=WOAB[0:65, h * 512:(h + 1) * 512],
                              start=True, stop=False)
                    te.matmul(bank, lhsT=IDF32[:, :],
                              rhs=X[:, t * D + h * 512: t * D + (h + 1) * 512],
                              start=False, stop=True).then_inc(s_y, 1)

            for c in range(0, 4):
                mm_h(c)
            mm_s(2)
            for c in range(4, 8):
                mm_h(c)
            mm_s(3)
            mm_h(8)
            mm_g(0)
            for c in range(9, 13):
                mm_h(c)
            mm_g(1)
            for c in range(13, 16):
                mm_h(c)
            for k in range(4):
                mm_y(0, k)
            mm_g(2)
            for k in range(4):
                mm_y(1, k)
            mm_g(3)
            for k in range(4):
                mm_y(2, k)
            for k in range(4):
                mm_y(3, k)

    return nc


def _layout_weights(inputs):
    wq = np.asarray(inputs["L0_attn_W_Q"], np.float32)
    wk = np.asarray(inputs["L0_attn_W_K"], np.float32)
    gam = np.asarray(inputs["L0_attn_gamma"], np.float32)
    bet = np.asarray(inputs["L0_attn_beta"], np.float32)
    win = np.asarray(inputs["L0_ffn_Win"], np.float32)
    binv = np.asarray(inputs["L0_ffn_bin_"], np.float32)
    w1 = np.asarray(inputs["L0_ffn_W1"], np.float32)
    b1 = np.asarray(inputs["L0_ffn_b1"], np.float32)
    w2 = np.asarray(inputs["L0_ffn_W2"], np.float32)
    b2 = np.asarray(inputs["L0_ffn_b2"], np.float32)
    wout = np.asarray(inputs["L0_ffn_Wout"], np.float32)
    bout = np.asarray(inputs["L0_ffn_bout"], np.float32)

    def tposed(w):
        return np.ascontiguousarray(
            w.T.reshape(8, 128, 128).transpose(1, 0, 2).reshape(128, 1024))

    bta = np.zeros((65, 8, 128), np.float32)
    cta = np.zeros((128, 8, 64), np.float32)
    for j in range(8):
        for cix in range(128):
            m = 8 * j + cix // 16
            i = cix % 16
            bta[m, j, cix] = w1[m, i]
            bta[64, j, cix] = b1[m, i]
            cta[cix, j, m] = w2[m, i]

    return {
        "wqt": tposed(wq),
        "wkt": tposed(wk),
        "gam": np.ascontiguousarray(np.broadcast_to(gam, (128, 1024))),
        "bet": np.ascontiguousarray(np.broadcast_to(bet, (128, 1024))),
        "win": np.ascontiguousarray(
            win.reshape(8, 128, 64).transpose(1, 0, 2).reshape(128, 512)),
        "bta": np.ascontiguousarray(bta.reshape(65, 1024)),
        "cta": np.ascontiguousarray(cta.reshape(128, 512)),
        "woa": np.ascontiguousarray(np.concatenate([wout, bout[None, :]], 0)),
        "b2c": np.ascontiguousarray(b2[None, :]),
        "binc": np.ascontiguousarray(binv[None, :]),
        "idb": np.eye(128, dtype=ml_dtypes.bfloat16),
    }


def kernel(**inputs):
    x_full = np.asarray(inputs["L0_attn_J"], np.float32)
    wmaps = _layout_weights(inputs)
    nc = build_nc()
    in_maps = []
    for c in range(N_CORES):
        m = dict(wmaps)
        m["x"] = np.ascontiguousarray(x_full[c])
        in_maps.append(m)
    res = run_bass_kernel_spmd(nc, in_maps, core_ids=list(range(N_CORES)))
    return np.stack([np.asarray(r["y"], np.float32) for r in res.results], 0)


# version 44
# speedup vs baseline: 1.0862x; 1.0862x over previous
import contextlib

import numpy as np
import ml_dtypes

import concourse.bass as bass
import concourse.mybir as mybir
from concourse.bass_utils import run_bass_kernel_spmd

F32 = mybir.dt.float32
BF16 = mybir.dt.bfloat16
AF = mybir.ActivationFunctionType
ALU = mybir.AluOpType

N_CORES = 8
N = 2048
D = 1024
T = 16
EPS = 1e-5
INV_SQRT_DK = 1.0 / np.sqrt(128.0)


def build_nc():
    nc = bass.Bass("TRN2", target_bir_lowering=False,
                   detect_race_conditions=False)

    x_d = nc.dram_tensor("x", [N, D], F32, kind="ExternalInput").ap()
    wqt_d = nc.dram_tensor("wqt", [128, 1024], F32, kind="ExternalInput").ap()
    wkt_d = nc.dram_tensor("wkt", [128, 1024], F32, kind="ExternalInput").ap()
    gam_d = nc.dram_tensor("gam", [128, 1024], F32, kind="ExternalInput").ap()
    bet_d = nc.dram_tensor("bet", [128, 1024], F32, kind="ExternalInput").ap()
    win_d = nc.dram_tensor("win", [128, 512], F32, kind="ExternalInput").ap()
    bta_d = nc.dram_tensor("bta", [65, 1024], F32, kind="ExternalInput").ap()
    cta_d = nc.dram_tensor("cta", [128, 512], F32, kind="ExternalInput").ap()
    woa_d = nc.dram_tensor("woa", [65, 1024], F32, kind="ExternalInput").ap()
    b2c_d = nc.dram_tensor("b2c", [1, 64], F32, kind="ExternalInput").ap()
    binc_d = nc.dram_tensor("binc", [1, 64], F32, kind="ExternalInput").ap()
    idb_d = nc.dram_tensor("idb", [128, 128], BF16, kind="ExternalInput").ap()
    y_d = nc.dram_tensor("y", [N, D], F32, kind="ExternalOutput").ap()

    with contextlib.ExitStack() as ctx:
        ent = ctx.enter_context
        X = ent(nc.sbuf_tensor("X", [128, T * D], F32))
        XNB = ent(nc.sbuf_tensor("XNB", [128, T * D], BF16))
        Z2B = ent(nc.sbuf_tensor("Z2B", [128, T * D], BF16))
        XT = ent(nc.sbuf_tensor("XT", [128, 8 * 2048], BF16,
                                is_transpose_dest=True))
        WB = ent(nc.sbuf_tensor("WB", [128, 2 * D], BF16))
        KB = ent(nc.sbuf_tensor("KB", [128, 2048], BF16))
        QTB = ent(nc.sbuf_tensor("QTB", [128, 2048], BF16))
        MB = ent(nc.sbuf_tensor("MB", [128, 1024], BF16))
        SST = ent(nc.sbuf_tensor("SST", [65, 2048], BF16))
        GT = ent(nc.sbuf_tensor("GT", [65, 2048], BF16))
        WQTB = ent(nc.sbuf_tensor("WQTB", [128, 1024], BF16))
        WKTB = ent(nc.sbuf_tensor("WKTB", [128, 1024], BF16))
        GAMB = ent(nc.sbuf_tensor("GAMB", [128, 1024], BF16))
        BETB = ent(nc.sbuf_tensor("BETB", [128, 1024], BF16))
        WINB = ent(nc.sbuf_tensor("WINB", [128, 512], BF16))
        BTAB = ent(nc.sbuf_tensor("BTAB", [65, 1024], BF16))
        CTAB = ent(nc.sbuf_tensor("CTAB", [128, 512], BF16))
        WOAB = ent(nc.sbuf_tensor("WOAB", [65, 1024], BF16))
        B2CB = ent(nc.sbuf_tensor("B2CB", [1, 64], BF16))
        BINCB = ent(nc.sbuf_tensor("BINCB", [1, 64], BF16))
        IDB = ent(nc.sbuf_tensor("IDB", [128, 128], BF16))
        ONESB = ent(nc.sbuf_tensor("ONESB", [1, 512], BF16))
        STG0 = ent(nc.sbuf_tensor("STG0", [128, 1024], F32))
        STG1 = ent(nc.sbuf_tensor("STG1", [128, 1024], F32))
        ST = ent(nc.sbuf_tensor("ST", [128, 136], F32))
        MPS = ent(nc.psum_tensor("MPS", [128, 1024], F32))
        KQ0 = ent(nc.psum_tensor("KQ0", [128, 512], F32))
        KQ1 = ent(nc.psum_tensor("KQ1", [128, 512], F32))
        JP0 = ent(nc.psum_tensor("JP0", [128, 1024], F32))
        JP1 = ent(nc.psum_tensor("JP1", [128, 1024], F32))
        sem = lambda name: ent(nc.semaphore(name))
        s_x = [sem(f"s_xin{k}") for k in range(8)]
        s_wl = [sem("s_wa"), sem("s_wb")]
        s_idb = sem("s_idb")
        s_wc = sem("s_wc")
        s_nb = sem("s_nb")
        s_trs = [sem("s_tr0"), sem("s_tr1")]
        s_vs = sem("s_vs")
        s_sq = sem("s_sq")
        s_kq = sem("s_kq")
        s_ev = sem("s_ev")
        s_m = sem("s_m")
        s_mev = sem("s_mev")
        s_j = sem("s_j")
        s_z2 = sem("s_z2")
        s_wb = sem("s_wbt")
        s_x1 = sem("s_x1")
        s_smm = sem("s_smm")
        s_sev = sem("s_sev")
        s_hmm = sem("s_hmm")
        s_hev = sem("s_hev")
        s_hbs = sem("s_hbs")
        s_hbv = sem("s_hbv")
        s_gmm = sem("s_gmm")
        s_gev = sem("s_gev")
        s_y = sem("s_y")
        s_yv = sem("s_yv")
        s_out = sem("s_out")
        blk = ent(nc.Block())
        JP = [JP0, JP1]
        HROT = [JP0, JP1, MPS]
        KQ = [KQ0, KQ1]

        NS1, NS2, NS3 = 6, 4, 4
        SLOT_LN1 = lambda s: 24 + 8 * (s % NS1)
        SLOT_LN2 = lambda t: 72 + 8 * (t % NS2)
        SLOT_LN3 = lambda t: 104 + 8 * (t % NS3)

        fin_order = []
        for i in range(T + 3):
            if 2 <= i <= T + 1:
                fin_order.append(("L1", i - 2))
        for t in range(T + 4):
            if t < T:
                fin_order.append(("L2", t))
            if 3 <= t <= T + 2:
                fin_order.append(("L3", t - 3))
        ses_val = {k: idx + 1 for idx, k in enumerate(fin_order)}

        wplan = [
            (wkt_d, WKTB, 128, 1024),
            (wqt_d, WQTB, 128, 1024),
            (gam_d, GAMB, 128, 1024),
            (bet_d, BETB, 128, 1024),
            (win_d, WINB, 128, 512),
            (bta_d, BTAB, 65, 1024),
            (cta_d, CTAB, 128, 512),
            (woa_d, WOAB, 65, 1024),
            (b2c_d, B2CB, 1, 64),
            (binc_d, BINCB, 1, 64),
        ]
        NW = len(wplan)
        xs_after = {1: [2, 3], 2: [4, 5], 3: [6, 7]}

        @blk.gpsimd
        def _(g):
            g.dma_start(X[:, 0:D], x_d[0:128, :]).then_inc(s_x[0], 16)
            g.dma_start(X[:, D:2 * D], x_d[128:256, :]).then_inc(s_x[0], 16)
            g.dma_start(IDB[:, :], idb_d).then_inc(s_idb, 16)
            for i, (dsrc, dst, rows, cols) in enumerate(wplan):
                stg = [STG0, STG1][i % 2]
                if i >= 2:
                    g.wait_ge(s_wc, i - 1)
                g.dma_start(stg[0:rows, 0:cols], dsrc).then_inc(s_wl[i % 2], 16)
                for t in xs_after.get(i, []):
                    g.dma_start(X[:, t * D:(t + 1) * D],
                                x_d[t * 128:(t + 1) * 128, :]
                                ).then_inc(s_x[t // 2], 16)
            for t in range(T):
                g.wait_ge(s_yv, 2 * (t + 1))
                g.dma_start(y_d[t * 128:(t + 1) * 128, :],
                            X[:, t * D:(t + 1) * D]).then_inc(s_out, 16)
            g.wait_ge(s_out, 16 * T)

        def tr_wait_vals(u_max):
            return [16 * ((u_max + 2 - p) // 2) for p in range(2)]

        @blk.sync
        def _(sy):
            tot = [0, 0]
            for u in range(2 * T):
                if u < 8:
                    t8 = 8 + u
                    sy.dma_start(X[:, t8 * D:(t8 + 1) * D],
                                 x_d[t8 * 128:(t8 + 1) * 128, :]
                                 ).then_inc(s_x[t8 // 2], 16)
                p = u % 2
                if u >= 2:
                    sy.wait_ge(s_trs[p], tot[p])
                sy.wait_ge(s_nb, u + 1)
                t = u % T
                srcb = XNB if u < T else Z2B
                out3 = XT[:, 0:8 * 2048].rearrange(
                    "p (j n) -> p j n", n=2048)[:, :, t * 128:(t + 1) * 128]
                sy.dma_start_transpose(
                    out3, srcb[:, t * D:(t + 1) * D]).then_inc(s_trs[p], 16)
                tot[p] += 16

        @blk.vector
        def _(v):
            pos = [0]
            lastw = {}
            GAP = 3

            def V(fn, reads=(), writes=()):
                need = max((lastw.get(k, -99) + 1 + GAP for k in reads),
                           default=0)
                while pos[0] < need:
                    v.engine_nop()
                    pos[0] += 1
                r = fn()
                pos[0] += 1
                for k in writes:
                    lastw[k] = pos[0] - 1
                return r

            def stats_ab(scr, h0, h1):
                V(lambda: v.bn_stats(ST[:, scr:scr + 6], h0),
                  writes=(("scr", scr),))
                V(lambda: v.bn_stats(ST[:, scr + 6:scr + 12], h1),
                  writes=(("scr", scr),))

            def stats_g(slot, scr, gate=None, gate2=None):
                if gate is not None:
                    v.wait_ge(s_sq, gate)
                if gate2 is not None:
                    v.wait_ge(*gate2)
                V(lambda: v.bn_aggr(
                    ST[:, slot:slot + 2],
                    ST[:, scr:scr + 12].rearrange("p (g s) -> p g s", g=2)),
                  reads=(("scr", scr),), writes=((slot, "mv"),))

            def stats_nvr(slot):
                V(lambda: v.tensor_scalar_mul(ST[:, slot + 4:slot + 5],
                                              ST[:, slot:slot + 1], -1.0),
                  reads=((slot, "mv"),), writes=((slot, "nm"),))
                V(lambda: v.tensor_scalar_add(ST[:, slot + 2:slot + 3],
                                              ST[:, slot + 1:slot + 2], EPS),
                  reads=((slot, "mv"),), writes=((slot, "vpe"),))
                V(lambda: v.reciprocal(ST[:, slot + 3:slot + 4],
                                       ST[:, slot + 2:slot + 3]
                                       ).then_inc(s_vs, 1),
                  reads=((slot, "vpe"),), writes=((slot, "inv"),))

            def ln_apply(ses, out_ap, in_ap, sem_):
                kind, u = ses
                slot = {"L1": SLOT_LN1, "L2": SLOT_LN2,
                        "L3": SLOT_LN3}[kind](u)
                v.wait_ge(s_sq, ses_val[ses])
                return V(lambda: v.scalar_tensor_tensor(
                    out_ap, in_ap, ST[:, slot + 4:slot + 5],
                    ST[:, slot + 5:slot + 6].broadcast_to((128, 1024)),
                    op0=ALU.add, op1=ALU.mult).then_inc(sem_, 1),
                    reads=((slot, "nm"),))

            V(lambda: v.memset(ONESB[:, :], 1.0))
            V(lambda: v.memset(SST[64:65, :], 1.0))
            V(lambda: v.memset(GT[64:65, :], 1.0))

            for i in range(T + 2):
                if i < T:
                    v.wait_ge(s_x[i // 2], 32)
                    stats_ab(12 * (i % 2), X[:, i * D: i * D + 512],
                             X[:, i * D + 512: i * D + 1024])
                if 1 <= i <= T:
                    s = i - 1
                    stats_g(SLOT_LN1(s), 12 * (s % 2),
                            gate=(ses_val[("L1", s - NS1)]
                                  if s >= NS1 else None))
                if 2 <= i <= T + 1:
                    stats_nvr(SLOT_LN1(i - 2))
                if 4 <= i <= T:
                    t = i - 4
                    ln_apply(("L1", t), XNB[:, t * D:(t + 1) * D],
                             X[:, t * D:(t + 1) * D], s_nb)
                if i == T:
                    for t in (T - 3, T - 2):
                        ln_apply(("L1", t), XNB[:, t * D:(t + 1) * D],
                                 X[:, t * D:(t + 1) * D], s_nb)
                if i == T + 1:
                    t = T - 1
                    ln_apply(("L1", t), XNB[:, t * D:(t + 1) * D],
                             X[:, t * D:(t + 1) * D], s_nb)

            JROT = [MPS, JP0, JP1]
            for t in range(T + 4):
                if t < T:
                    p = JROT[t % 3]
                    v.wait_ge(s_j, t + 1)
                    stats_ab(0, p[:, 0:512], p[:, 512:1024])
                if 3 <= t <= T + 2:
                    u3 = t - 3
                    stats_ab(12, X[:, u3 * D: u3 * D + 512],
                             X[:, u3 * D + 512: u3 * D + 1024])
                if t < T:
                    stats_g(SLOT_LN2(t), 0,
                            gate2=((s_z2, t - 3) if t >= NS2 else None))
                if 3 <= t <= T + 2:
                    stats_g(SLOT_LN3(t - 3), 12,
                            gate=(ses_val[("L3", t - 3 - NS3)]
                                  if t - 3 >= NS3 else None))
                if t < T:
                    stats_nvr(SLOT_LN2(t))
                if 3 <= t <= T + 2:
                    stats_nvr(SLOT_LN3(t - 3))
                if 2 <= t <= T + 1:
                    u = t - 2
                    v.wait_ge(s_z2, u + 1)
                    wsl = WB[:, (u % 2) * D:(u % 2 + 1) * D]
                    V(lambda: v.tensor_tensor(wsl, Z2B[:, u * D:(u + 1) * D],
                                              GAMB[:, :], op=ALU.mult))
                    V(lambda: v.tensor_tensor(wsl, wsl, BETB[:, :],
                                              op=ALU.add))
                    V(lambda: v.tensor_tensor(X[:, u * D:(u + 1) * D],
                                              X[:, u * D:(u + 1) * D], wsl,
                                              op=ALU.add).then_inc(s_x1, 1))

            def relu_b(c):
                jp = HROT[c % 3]
                v.wait_ge(s_hmm, c + 1)
                last = None
                for j in range(4, 8):
                    last = v.tensor_scalar_max(
                        XNB[:, j * 2048 + c * 128: j * 2048 + (c + 1) * 128],
                        jp[:, j * 128:(j + 1) * 128], 0.0)
                last.then_inc(s_hbv, 1)

            def y_asm(t):
                kqy = t >= 8 and t % 2 == 1
                for h in range(2):
                    v.wait_ge(s_y, 2 * t + h + 1)
                    src_ = (KQ0[:, :] if h == 0 else KQ1[:, :]) if kqy \
                        else MPS[:, h * 512:(h + 1) * 512]
                    v.scalar_tensor_tensor(
                        X[:, t * D + h * 512: t * D + (h + 1) * 512],
                        src_, 1.0,
                        X[:, t * D + h * 512: t * D + (h + 1) * 512],
                        op0=ALU.mult, op1=ALU.add).then_inc(s_yv, 1)

            for c in range(4, 16):
                relu_b(c)
            for t in range(16):
                y_asm(t)

        @blk.scalar
        def _(s):
            s.preload_act_table(AF.Sqrt)
            SLOT = {"L1": SLOT_LN1, "L2": SLOT_LN2, "L3": SLOT_LN3}

            def emit_cast(i):
                dsrc, dst, rows, cols = wplan[i]
                stg = [STG0, STG1][i % 2]
                s.wait_ge(s_wl[i % 2], 16 * (i // 2 + 1))
                s.activation(dst[0:rows, 0:cols], stg[0:rows, 0:cols],
                             AF.Copy).then_inc(s_wc, 1)

            def sev(cc):
                s.wait_ge(s_smm, cc + 1)
                s.activation(SST[0:64, cc * 512:(cc + 1) * 512],
                             KQ[cc % 2][0:64, :], AF.Copy).then_inc(s_sev, 1)

            emit_cast(0)
            emit_cast(1)

            def kq_evac(t):
                s.wait_ge(s_kq, t + 1)
                kq = KQ[t % 2]
                s.activation(KB[:, t * 128:(t + 1) * 128], kq[:, 0:128],
                             AF.Copy)
                s.activation(QTB[:, t * 128:(t + 1) * 128], kq[:, 128:256],
                             AF.Copy, scale=INV_SQRT_DK).then_inc(s_ev, 1)

            def relu_a(c, full=False):
                jp = HROT[c % 3]
                s.wait_ge(s_hmm, c + 1)
                last = None
                for j in range(0, 4):
                    last = s.activation(
                        XNB[:, j * 2048 + c * 128: j * 2048 + (c + 1) * 128],
                        jp[:, j * 128:(j + 1) * 128], AF.Relu)
                last.then_inc(s_hev, 1)
                if full:
                    for j in range(4, 8):
                        last = s.activation(
                            XNB[:, j * 2048 + c * 128: j * 2048 + (c + 1) * 128],
                            jp[:, j * 128:(j + 1) * 128], AF.Relu)
                    last.then_inc(s_hbs, 1)

            pend_nr = []
            pend_ap = []
            JROT_S = [MPS, JP0, JP1]

            def ln_flush(pad=0):
                if pend_ap:
                    kind, u = pend_ap.pop(0)
                    slot = SLOT[kind](u)
                    for _ in range(pad):
                        s.nop()
                    if kind == "L3":
                        s.activation(Z2B[:, u * D:(u + 1) * D],
                                     X[:, u * D:(u + 1) * D], AF.Identity,
                                     bias=ST[:, slot + 6:slot + 7],
                                     scale=ST[:, slot + 5:slot + 6]
                                     ).then_inc(s_nb, 1)
                    else:
                        s.activation(Z2B[:, u * D:(u + 1) * D],
                                     JROT_S[u % 3][:, :], AF.Identity,
                                     bias=ST[:, slot + 6:slot + 7],
                                     scale=ST[:, slot + 5:slot + 6]
                                     ).then_inc(s_z2, 1)
                if pend_nr:
                    kind, u = pend_nr.pop(0)
                    slot = SLOT[kind](u)
                    s.activation(ST[:, slot + 6:slot + 7],
                                 ST[:, slot + 4:slot + 5], AF.Identity,
                                 bias=0.0, scale=ST[:, slot + 5:slot + 6])
                    pend_ap.append((kind, u))

            for ses in fin_order:
                kind, idx = ses
                slot = SLOT[kind](idx)
                s.wait_ge(s_vs, ses_val[ses])
                s.activation(ST[:, slot + 5:slot + 6],
                             ST[:, slot + 3:slot + 4],
                             AF.Sqrt).then_inc(s_sq, 1)
                ln_flush()
                if kind == "L1" and idx < 8:
                    emit_cast(idx + 2)
                if kind != "L1":
                    pend_nr.append((kind, idx))
                if kind == "L1" and idx >= 3:
                    kq_evac(idx - 3)
                if ses == ("L1", T - 1):
                    for tt in (T - 3, T - 2, T - 1):
                        kq_evac(tt)
                    s.wait_ge(s_m, 1)
                    s.activation(MB[:, 0:512], MPS[:, 0:512], AF.Copy)
                    s.activation(MB[:, 512:1024], MPS[:, 512:1024],
                                 AF.Copy).then_inc(s_mev, 1)
                if ses == ("L3", 12):
                    sev(0)
                    sev(1)
                if ses == ("L3", 13):
                    relu_a(0, full=True)
                if ses == ("L3", 14):
                    relu_a(1, full=True)
                if ses == ("L3", 15):
                    sev(2)
            while pend_nr or pend_ap:
                ln_flush(pad=4)

            def gev(cc):
                s.wait_ge(s_gmm, cc + 1)
                s.activation(GT[0:64, cc * 512:(cc + 1) * 512],
                             KQ[cc % 2][0:64, :],
                             AF.Copy).then_inc(s_gev, 1)

            relu_a(2, full=True)
            relu_a(3, full=True)
            relu_a(4)
            relu_a(5)
            relu_a(6)
            relu_a(7)
            sev(3)
            gev(0)
            for c in range(8, 12):
                relu_a(c)
            gev(1)
            for c in range(12, 16):
                relu_a(c)
            gev(2)
            gev(3)

        @blk.tensor
        def _(te):
            te.wait_ge(s_wc, 2)
            te.wait_ge(s_idb, 16)

            def xt_t(j, t):
                return XT[:, j * 2048 + t * 128: j * 2048 + (t + 1) * 128]

            for t in range(T):
                te.wait_ge(s_trs[t % 2], 16 * (t // 2 + 1))
                if t >= 2:
                    te.wait_ge(s_ev, t - 1)
                kq = KQ[t % 2]
                for j in range(8):
                    te.matmul(kq[:, 0:128], lhsT=xt_t(j, t),
                              rhs=WKTB[:, j * 128:(j + 1) * 128],
                              start=(j == 0), stop=False)
                last = None
                for j in range(8):
                    last = te.matmul(kq[:, 128:256],
                                     lhsT=WQTB[:, j * 128:(j + 1) * 128],
                                     rhs=xt_t(j, t),
                                     start=False, stop=(j == 7))
                last.then_inc(s_kq, 1)
                te.wait_ge(s_ev, t + 1)
                for h in range(2):
                    last = te.matmul(
                        MPS[:, h * 512:(h + 1) * 512],
                        lhsT=KB[:, t * 128:(t + 1) * 128],
                        rhs=XNB[:, t * D + h * 512: t * D + (h + 1) * 512],
                        start=(t == 0), stop=(t == 15))
            last.then_inc(s_m, 1)

            def mm_s(cc):
                wv = tr_wait_vals(16 + 4 * cc + 3)
                te.wait_ge(s_trs[0], wv[0])
                te.wait_ge(s_trs[1], wv[1])
                if cc == 2:
                    te.wait_ge(s_sev, 1)
                elif cc == 3:
                    te.wait_ge(s_sev, 2)
                sp = KQ[cc % 2]
                for j in range(8):
                    te.matmul(sp[0:64, 0:512],
                              lhsT=WINB[:, j * 64:(j + 1) * 64],
                              rhs=XT[:, j * 2048 + cc * 512: j * 2048 + (cc + 1) * 512],
                              start=(j == 0), stop=False)
                te.matmul(sp[0:64, 0:512], lhsT=BINCB[:, :],
                          rhs=ONESB[:, :],
                          start=False, stop=True).then_inc(s_smm, 1)

            def warm(bank):
                te.matmul(bank[0:64, 0:64], lhsT=ONESB[:, 0:64],
                          rhs=ONESB[:, 0:64], start=True, stop=True)

            te.wait_ge(s_vs, ses_val[("L1", T - 1)])
            warm(JP1)
            te.wait_ge(s_mev, 1)
            te.wait_ge(s_wc, NW)
            te.wait_ge(s_ev, T)
            JROT = [MPS, JP0, JP1]
            for t in range(T):
                if t >= 4:
                    te.wait_ge(s_vs, ses_val[("L3", t - 4)])
                    warm(KQ0)
                if t >= 3:
                    te.wait_ge(s_z2, t - 2)
                p = JROT[t % 3]
                for h in range(2):
                    te.matmul(p[:, h * 512:(h + 1) * 512],
                              lhsT=QTB[:, t * 128:(t + 1) * 128],
                              rhs=MB[:, h * 512:(h + 1) * 512],
                              start=True, stop=False)
                    last = te.matmul(
                        p[:, h * 512:(h + 1) * 512],
                        lhsT=IDB[:, :],
                        rhs=XNB[:, t * D + h * 512: t * D + (h + 1) * 512],
                        start=False, stop=True)
                last.then_inc(s_j, 1)
            mm_s(0)
            mm_s(1)

            def mm_h(c):
                te.wait_ge(s_sev, c // 4 + 1)
                if c == 0:
                    te.wait_ge(s_z2, 14)
                elif c == 1:
                    te.wait_ge(s_z2, 15)
                elif c == 2:
                    te.wait_ge(s_z2, T)
                else:
                    cp = c - 3
                    te.wait_ge(s_hev, cp + 1)
                    te.wait_ge(s_hbs, min(cp + 1, 4))
                    if cp >= 4:
                        te.wait_ge(s_hbv, cp - 3)
                hp = HROT[c % 3]
                last = None
                for j in range(8):
                    last = te.matmul(hp[:, j * 128:(j + 1) * 128],
                                     lhsT=BTAB[0:65, j * 128:(j + 1) * 128],
                                     rhs=SST[0:65, c * 128:(c + 1) * 128],
                                     start=True, stop=True)
                last.then_inc(s_hmm, 1)

            def mm_g(cc):
                te.wait_ge(s_hev, 4 * cc + 4)
                te.wait_ge(s_hbs, 4)
                if cc >= 1:
                    te.wait_ge(s_hbv, 4 * cc)
                if cc < 2:
                    te.wait_ge(s_sev, cc + 3)
                gv = KQ[cc % 2][0:64, :]
                for j in range(8):
                    te.matmul(gv, lhsT=CTAB[:, j * 64:(j + 1) * 64],
                              rhs=XNB[:, j * 2048 + cc * 512: j * 2048 + (cc + 1) * 512],
                              start=(j == 0), stop=False)
                te.matmul(gv, lhsT=B2CB[:, :],
                          rhs=ONESB[:, :],
                          start=False, stop=True).then_inc(s_gmm, 1)

            def mm_y(cc, k):
                t = 4 * cc + k
                kqy = t >= 8 and t % 2 == 1
                if t == 0:
                    te.wait_ge(s_hev, 15)
                    te.wait_ge(s_hbv, 11)
                te.wait_ge(s_gev, 4 if kqy else cc + 1)
                for h in range(2):
                    if t >= 10:
                        te.wait_ge(s_yv, 2 * (t - 2) + h + 1)
                    elif t == 8 or (1 <= t < 8):
                        te.wait_ge(s_yv, 2 * (t - 1) + h + 1)
                    bank = (KQ0[:, :] if h == 0 else KQ1[:, :]) if kqy \
                        else MPS[:, h * 512:(h + 1) * 512]
                    te.matmul(bank,
                              lhsT=GT[0:65, t * 128:(t + 1) * 128],
                              rhs=WOAB[0:65, h * 512:(h + 1) * 512],
                              start=True, stop=True).then_inc(s_y, 1)

            for c in range(0, 4):
                mm_h(c)
            mm_s(2)
            for c in range(4, 8):
                mm_h(c)
            mm_s(3)
            mm_h(8)
            mm_g(0)
            for c in range(9, 13):
                mm_h(c)
            mm_g(1)
            for c in range(13, 16):
                mm_h(c)
            for k in range(4):
                mm_y(0, k)
            mm_g(2)
            for k in range(4):
                mm_y(1, k)
            mm_g(3)
            for k in range(4):
                mm_y(2, k)
            for k in range(4):
                mm_y(3, k)

    return nc


def _layout_weights(inputs):
    wq = np.asarray(inputs["L0_attn_W_Q"], np.float32)
    wk = np.asarray(inputs["L0_attn_W_K"], np.float32)
    gam = np.asarray(inputs["L0_attn_gamma"], np.float32)
    bet = np.asarray(inputs["L0_attn_beta"], np.float32)
    win = np.asarray(inputs["L0_ffn_Win"], np.float32)
    binv = np.asarray(inputs["L0_ffn_bin_"], np.float32)
    w1 = np.asarray(inputs["L0_ffn_W1"], np.float32)
    b1 = np.asarray(inputs["L0_ffn_b1"], np.float32)
    w2 = np.asarray(inputs["L0_ffn_W2"], np.float32)
    b2 = np.asarray(inputs["L0_ffn_b2"], np.float32)
    wout = np.asarray(inputs["L0_ffn_Wout"], np.float32)
    bout = np.asarray(inputs["L0_ffn_bout"], np.float32)

    def tposed(w):
        return np.ascontiguousarray(
            w.T.reshape(8, 128, 128).transpose(1, 0, 2).reshape(128, 1024))

    bta = np.zeros((65, 8, 128), np.float32)
    cta = np.zeros((128, 8, 64), np.float32)
    for j in range(8):
        for cix in range(128):
            m = 8 * j + cix // 16
            i = cix % 16
            bta[m, j, cix] = w1[m, i]
            bta[64, j, cix] = b1[m, i]
            cta[cix, j, m] = w2[m, i]

    return {
        "wqt": tposed(wq),
        "wkt": tposed(wk),
        "gam": np.ascontiguousarray(np.broadcast_to(gam, (128, 1024))),
        "bet": np.ascontiguousarray(np.broadcast_to(bet, (128, 1024))),
        "win": np.ascontiguousarray(
            win.reshape(8, 128, 64).transpose(1, 0, 2).reshape(128, 512)),
        "bta": np.ascontiguousarray(bta.reshape(65, 1024)),
        "cta": np.ascontiguousarray(cta.reshape(128, 512)),
        "woa": np.ascontiguousarray(np.concatenate([wout, bout[None, :]], 0)),
        "b2c": np.ascontiguousarray(b2[None, :]),
        "binc": np.ascontiguousarray(binv[None, :]),
        "idb": np.eye(128, dtype=ml_dtypes.bfloat16),
    }


def kernel(**inputs):
    x_full = np.asarray(inputs["L0_attn_J"], np.float32)
    wmaps = _layout_weights(inputs)
    nc = build_nc()
    in_maps = []
    for c in range(N_CORES):
        m = dict(wmaps)
        m["x"] = np.ascontiguousarray(x_full[c])
        in_maps.append(m)
    res = run_bass_kernel_spmd(nc, in_maps, core_ids=list(range(N_CORES)))
    return np.stack([np.asarray(r["y"], np.float32) for r in res.results], 0)


# version 45
# speedup vs baseline: 1.0979x; 1.0108x over previous
import contextlib

import numpy as np
import ml_dtypes

import concourse.bass as bass
import concourse.mybir as mybir
from concourse.bass_utils import run_bass_kernel_spmd

F32 = mybir.dt.float32
BF16 = mybir.dt.bfloat16
AF = mybir.ActivationFunctionType
ALU = mybir.AluOpType

N_CORES = 8
N = 2048
D = 1024
T = 16
EPS = 1e-5
INV_SQRT_DK = 1.0 / np.sqrt(128.0)


def build_nc():
    nc = bass.Bass("TRN2", target_bir_lowering=False,
                   detect_race_conditions=False)

    x_d = nc.dram_tensor("x", [N, D], F32, kind="ExternalInput").ap()
    wqt_d = nc.dram_tensor("wqt", [128, 1024], F32, kind="ExternalInput").ap()
    wkt_d = nc.dram_tensor("wkt", [128, 1024], F32, kind="ExternalInput").ap()
    gam_d = nc.dram_tensor("gam", [128, 1024], F32, kind="ExternalInput").ap()
    bet_d = nc.dram_tensor("bet", [128, 1024], F32, kind="ExternalInput").ap()
    win_d = nc.dram_tensor("win", [128, 512], F32, kind="ExternalInput").ap()
    bta_d = nc.dram_tensor("bta", [65, 1024], F32, kind="ExternalInput").ap()
    cta_d = nc.dram_tensor("cta", [128, 512], F32, kind="ExternalInput").ap()
    woa_d = nc.dram_tensor("woa", [65, 1024], F32, kind="ExternalInput").ap()
    b2c_d = nc.dram_tensor("b2c", [1, 64], F32, kind="ExternalInput").ap()
    binc_d = nc.dram_tensor("binc", [1, 64], F32, kind="ExternalInput").ap()
    idb_d = nc.dram_tensor("idb", [128, 128], BF16, kind="ExternalInput").ap()
    y_d = nc.dram_tensor("y", [N, D], F32, kind="ExternalOutput").ap()

    with contextlib.ExitStack() as ctx:
        ent = ctx.enter_context
        X = ent(nc.sbuf_tensor("X", [128, T * D], F32))
        XNB = ent(nc.sbuf_tensor("XNB", [128, T * D], BF16))
        Z2B = ent(nc.sbuf_tensor("Z2B", [128, T * D], BF16))
        XT = ent(nc.sbuf_tensor("XT", [128, 8 * 2048], BF16,
                                is_transpose_dest=True))
        WB = ent(nc.sbuf_tensor("WB", [128, 2 * D], BF16))
        KB = ent(nc.sbuf_tensor("KB", [128, 2048], BF16))
        QTB = ent(nc.sbuf_tensor("QTB", [128, 2048], BF16))
        MB = ent(nc.sbuf_tensor("MB", [128, 1024], BF16))
        SST = ent(nc.sbuf_tensor("SST", [65, 2048], BF16))
        GT = ent(nc.sbuf_tensor("GT", [65, 2048], BF16))
        WQTB = ent(nc.sbuf_tensor("WQTB", [128, 1024], BF16))
        WKTB = ent(nc.sbuf_tensor("WKTB", [128, 1024], BF16))
        GAMB = ent(nc.sbuf_tensor("GAMB", [128, 1024], BF16))
        BETB = ent(nc.sbuf_tensor("BETB", [128, 1024], BF16))
        WINB = ent(nc.sbuf_tensor("WINB", [128, 512], BF16))
        BTAB = ent(nc.sbuf_tensor("BTAB", [65, 1024], BF16))
        CTAB = ent(nc.sbuf_tensor("CTAB", [128, 512], BF16))
        WOAB = ent(nc.sbuf_tensor("WOAB", [65, 1024], BF16))
        B2CB = ent(nc.sbuf_tensor("B2CB", [1, 64], BF16))
        BINCB = ent(nc.sbuf_tensor("BINCB", [1, 64], BF16))
        IDB = ent(nc.sbuf_tensor("IDB", [128, 128], BF16))
        ONESB = ent(nc.sbuf_tensor("ONESB", [1, 512], BF16))
        STG0 = ent(nc.sbuf_tensor("STG0", [128, 1024], F32))
        STG1 = ent(nc.sbuf_tensor("STG1", [128, 1024], F32))
        ST = ent(nc.sbuf_tensor("ST", [128, 136], F32))
        MPS = ent(nc.psum_tensor("MPS", [128, 1024], F32))
        KQ0 = ent(nc.psum_tensor("KQ0", [128, 512], F32))
        KQ1 = ent(nc.psum_tensor("KQ1", [128, 512], F32))
        JP0 = ent(nc.psum_tensor("JP0", [128, 1024], F32))
        JP1 = ent(nc.psum_tensor("JP1", [128, 1024], F32))
        sem = lambda name: ent(nc.semaphore(name))
        s_x = [sem(f"s_xin{k}") for k in range(8)]
        s_wl = [sem("s_wa"), sem("s_wb")]
        s_idb = sem("s_idb")
        s_wc = sem("s_wc")
        s_nb = sem("s_nb")
        s_trs = [sem("s_tr0"), sem("s_tr1")]
        s_vs = sem("s_vs")
        s_sq = sem("s_sq")
        s_kq = sem("s_kq")
        s_ev = sem("s_ev")
        s_m = sem("s_m")
        s_mev = sem("s_mev")
        s_j = sem("s_j")
        s_z2 = sem("s_z2")
        s_wb = sem("s_wbt")
        s_x1 = sem("s_x1")
        s_smm = sem("s_smm")
        s_sev = sem("s_sev")
        s_hmm = sem("s_hmm")
        s_hev = sem("s_hev")
        s_hbs = sem("s_hbs")
        s_hbv = sem("s_hbv")
        s_gmm = sem("s_gmm")
        s_gev = sem("s_gev")
        s_y = sem("s_y")
        s_yv = sem("s_yv")
        s_out = sem("s_out")
        blk = ent(nc.Block())
        JP = [JP0, JP1]
        HROT = [JP0, JP1, MPS]
        KQ = [KQ0, KQ1]

        NS1, NS2, NS3 = 6, 4, 4
        SLOT_LN1 = lambda s: 24 + 8 * (s % NS1)
        SLOT_LN2 = lambda t: 72 + 8 * (t % NS2)
        SLOT_LN3 = lambda t: 104 + 8 * (t % NS3)

        fin_order = []
        for i in range(T + 3):
            if 2 <= i <= T + 1:
                fin_order.append(("L1", i - 2))
        for t in range(T + 4):
            if t < T:
                fin_order.append(("L2", t))
            if 3 <= t <= T + 2:
                fin_order.append(("L3", t - 3))
        ses_val = {k: idx + 1 for idx, k in enumerate(fin_order)}

        wplan = [
            (wkt_d, WKTB, 128, 1024),
            (wqt_d, WQTB, 128, 1024),
            (gam_d, GAMB, 128, 1024),
            (bet_d, BETB, 128, 1024),
            (win_d, WINB, 128, 512),
            (bta_d, BTAB, 65, 1024),
            (cta_d, CTAB, 128, 512),
            (woa_d, WOAB, 65, 1024),
            (b2c_d, B2CB, 1, 64),
            (binc_d, BINCB, 1, 64),
        ]
        NW = len(wplan)
        xs_after = {1: [2, 3], 2: [4, 5], 3: [6, 7]}

        @blk.gpsimd
        def _(g):
            g.dma_start(X[:, 0:D], x_d[0:128, :]).then_inc(s_x[0], 16)
            g.dma_start(X[:, D:2 * D], x_d[128:256, :]).then_inc(s_x[0], 16)
            g.dma_start(IDB[:, :], idb_d).then_inc(s_idb, 16)
            for i, (dsrc, dst, rows, cols) in enumerate(wplan):
                stg = [STG0, STG1][i % 2]
                if i >= 2:
                    g.wait_ge(s_wc, i - 1)
                g.dma_start(stg[0:rows, 0:cols], dsrc).then_inc(s_wl[i % 2], 16)
                for t in xs_after.get(i, []):
                    g.dma_start(X[:, t * D:(t + 1) * D],
                                x_d[t * 128:(t + 1) * 128, :]
                                ).then_inc(s_x[t // 2], 16)
            for t in range(T):
                g.wait_ge(s_yv, 2 * (t + 1))
                g.dma_start(y_d[t * 128:(t + 1) * 128, :],
                            X[:, t * D:(t + 1) * D]).then_inc(s_out, 16)
            g.wait_ge(s_out, 16 * T)

        def tr_wait_vals(u_max):
            return [16 * ((u_max + 2 - p) // 2) for p in range(2)]

        @blk.sync
        def _(sy):
            tot = [0, 0]
            for u in range(2 * T):
                if u < 8:
                    t8 = 8 + u
                    sy.dma_start(X[:, t8 * D:(t8 + 1) * D],
                                 x_d[t8 * 128:(t8 + 1) * 128, :]
                                 ).then_inc(s_x[t8 // 2], 16)
                p = u % 2
                if u >= 2:
                    sy.wait_ge(s_trs[p], tot[p])
                sy.wait_ge(s_nb, u + 1)
                t = u % T
                srcb = XNB if u < T else Z2B
                out3 = XT[:, 0:8 * 2048].rearrange(
                    "p (j n) -> p j n", n=2048)[:, :, t * 128:(t + 1) * 128]
                sy.dma_start_transpose(
                    out3, srcb[:, t * D:(t + 1) * D]).then_inc(s_trs[p], 16)
                tot[p] += 16

        @blk.vector
        def _(v):
            pos = [0]
            lastw = {}
            GAP = 3

            def V(fn, reads=(), writes=()):
                need = max((lastw.get(k, -99) + 1 + GAP for k in reads),
                           default=0)
                while pos[0] < need:
                    v.engine_nop()
                    pos[0] += 1
                r = fn()
                pos[0] += 1
                for k in writes:
                    lastw[k] = pos[0] - 1
                return r

            def stats_ab(scr, h0, h1):
                V(lambda: v.bn_stats(ST[:, scr:scr + 6], h0),
                  writes=(("scr", scr),))
                V(lambda: v.bn_stats(ST[:, scr + 6:scr + 12], h1),
                  writes=(("scr", scr),))

            def stats_g(slot, scr, gate=None, gate2=None):
                if gate is not None:
                    v.wait_ge(s_sq, gate)
                if gate2 is not None:
                    v.wait_ge(*gate2)
                V(lambda: v.bn_aggr(
                    ST[:, slot:slot + 2],
                    ST[:, scr:scr + 12].rearrange("p (g s) -> p g s", g=2)),
                  reads=(("scr", scr),), writes=((slot, "mv"),))

            def stats_nvr(slot):
                V(lambda: v.tensor_scalar_mul(ST[:, slot + 4:slot + 5],
                                              ST[:, slot:slot + 1], -1.0),
                  reads=((slot, "mv"),), writes=((slot, "nm"),))
                V(lambda: v.tensor_scalar_add(ST[:, slot + 2:slot + 3],
                                              ST[:, slot + 1:slot + 2], EPS),
                  reads=((slot, "mv"),), writes=((slot, "vpe"),))
                V(lambda: v.reciprocal(ST[:, slot + 3:slot + 4],
                                       ST[:, slot + 2:slot + 3]
                                       ).then_inc(s_vs, 1),
                  reads=((slot, "vpe"),), writes=((slot, "inv"),))

            def ln_apply(ses, out_ap, in_ap, sem_):
                kind, u = ses
                slot = {"L1": SLOT_LN1, "L2": SLOT_LN2,
                        "L3": SLOT_LN3}[kind](u)
                v.wait_ge(s_sq, ses_val[ses])
                return V(lambda: v.scalar_tensor_tensor(
                    out_ap, in_ap, ST[:, slot + 4:slot + 5],
                    ST[:, slot + 5:slot + 6].broadcast_to((128, 1024)),
                    op0=ALU.add, op1=ALU.mult).then_inc(sem_, 1),
                    reads=((slot, "nm"),))

            V(lambda: v.memset(ONESB[:, :], 1.0))
            V(lambda: v.memset(SST[64:65, :], 1.0))
            V(lambda: v.memset(GT[64:65, :], 1.0))

            for i in range(T + 2):
                if i < T:
                    v.wait_ge(s_x[i // 2], 32)
                    stats_ab(12 * (i % 2), X[:, i * D: i * D + 512],
                             X[:, i * D + 512: i * D + 1024])
                if 1 <= i <= T:
                    s = i - 1
                    stats_g(SLOT_LN1(s), 12 * (s % 2),
                            gate=(ses_val[("L1", s - NS1)]
                                  if s >= NS1 else None))
                if 2 <= i <= T + 1:
                    stats_nvr(SLOT_LN1(i - 2))
                if 4 <= i <= T:
                    t = i - 4
                    ln_apply(("L1", t), XNB[:, t * D:(t + 1) * D],
                             X[:, t * D:(t + 1) * D], s_nb)
                if i == T:
                    for t in (T - 3, T - 2):
                        ln_apply(("L1", t), XNB[:, t * D:(t + 1) * D],
                                 X[:, t * D:(t + 1) * D], s_nb)
                if i == T + 1:
                    t = T - 1
                    ln_apply(("L1", t), XNB[:, t * D:(t + 1) * D],
                             X[:, t * D:(t + 1) * D], s_nb)

            JROT = [MPS, JP0, JP1]
            for t in range(T + 4):
                if t < T:
                    p = JROT[t % 3]
                    v.wait_ge(s_j, t + 1)
                    stats_ab(0, p[:, 0:512], p[:, 512:1024])
                if 3 <= t <= T + 2:
                    u3 = t - 3
                    stats_ab(12, X[:, u3 * D: u3 * D + 512],
                             X[:, u3 * D + 512: u3 * D + 1024])
                if t < T:
                    stats_g(SLOT_LN2(t), 0,
                            gate2=((s_z2, t - 3) if t >= NS2 else None))
                if 3 <= t <= T + 2:
                    stats_g(SLOT_LN3(t - 3), 12,
                            gate=(ses_val[("L3", t - 3 - NS3)]
                                  if t - 3 >= NS3 else None))
                if t < T:
                    stats_nvr(SLOT_LN2(t))
                if 3 <= t <= T + 2:
                    stats_nvr(SLOT_LN3(t - 3))
                if 2 <= t <= T + 1:
                    u = t - 2
                    v.wait_ge(s_z2, u + 1)
                    wsl = WB[:, (u % 2) * D:(u % 2 + 1) * D]
                    V(lambda: v.tensor_tensor(wsl, Z2B[:, u * D:(u + 1) * D],
                                              GAMB[:, :], op=ALU.mult))
                    V(lambda: v.tensor_tensor(wsl, wsl, BETB[:, :],
                                              op=ALU.add))
                    V(lambda: v.tensor_tensor(X[:, u * D:(u + 1) * D],
                                              X[:, u * D:(u + 1) * D], wsl,
                                              op=ALU.add).then_inc(s_x1, 1))

            def relu_b(c):
                jp = HROT[c % 3]
                v.wait_ge(s_hmm, c + 1)
                last = None
                for j in range(4, 8):
                    last = v.tensor_scalar_max(
                        XNB[:, j * 2048 + c * 128: j * 2048 + (c + 1) * 128],
                        jp[:, j * 128:(j + 1) * 128], 0.0)
                last.then_inc(s_hbv, 1)

            def y_asm(t):
                kqy = t >= 8 and t % 2 == 1
                for h in range(2):
                    v.wait_ge(s_y, 2 * t + h + 1)
                    src_ = (KQ0[:, :] if h == 0 else KQ1[:, :]) if kqy \
                        else MPS[:, h * 512:(h + 1) * 512]
                    v.scalar_tensor_tensor(
                        X[:, t * D + h * 512: t * D + (h + 1) * 512],
                        src_, 1.0,
                        X[:, t * D + h * 512: t * D + (h + 1) * 512],
                        op0=ALU.mult, op1=ALU.add).then_inc(s_yv, 1)

            for c in range(4, 16):
                relu_b(c)
            for t in range(16):
                y_asm(t)

        @blk.scalar
        def _(s):
            s.preload_act_table(AF.Sqrt)
            SLOT = {"L1": SLOT_LN1, "L2": SLOT_LN2, "L3": SLOT_LN3}

            def emit_cast(i):
                dsrc, dst, rows, cols = wplan[i]
                stg = [STG0, STG1][i % 2]
                s.wait_ge(s_wl[i % 2], 16 * (i // 2 + 1))
                s.activation(dst[0:rows, 0:cols], stg[0:rows, 0:cols],
                             AF.Copy).then_inc(s_wc, 1)

            def sev(cc):
                s.wait_ge(s_smm, cc + 1)
                s.activation(SST[0:64, cc * 512:(cc + 1) * 512],
                             KQ[cc % 2][0:64, :], AF.Copy).then_inc(s_sev, 1)

            emit_cast(0)
            emit_cast(1)

            def kq_evac(t):
                s.wait_ge(s_kq, t + 1)
                kq = KQ[t % 2]
                s.activation(KB[:, t * 128:(t + 1) * 128], kq[:, 0:128],
                             AF.Copy)
                s.activation(QTB[:, t * 128:(t + 1) * 128], kq[:, 128:256],
                             AF.Copy, scale=INV_SQRT_DK).then_inc(s_ev, 1)

            def relu_a(c, full=False):
                jp = HROT[c % 3]
                s.wait_ge(s_hmm, c + 1)
                last = None
                for j in range(0, 4):
                    last = s.activation(
                        XNB[:, j * 2048 + c * 128: j * 2048 + (c + 1) * 128],
                        jp[:, j * 128:(j + 1) * 128], AF.Relu)
                last.then_inc(s_hev, 1)
                if full:
                    for j in range(4, 8):
                        last = s.activation(
                            XNB[:, j * 2048 + c * 128: j * 2048 + (c + 1) * 128],
                            jp[:, j * 128:(j + 1) * 128], AF.Relu)
                    last.then_inc(s_hbs, 1)

            pend_nr = []
            pend_ap = []
            JROT_S = [MPS, JP0, JP1]

            def ln_flush(pad=0):
                if pend_ap:
                    kind, u = pend_ap.pop(0)
                    slot = SLOT[kind](u)
                    for _ in range(pad):
                        s.nop()
                    if kind == "L3":
                        s.activation(Z2B[:, u * D:(u + 1) * D],
                                     X[:, u * D:(u + 1) * D], AF.Identity,
                                     bias=ST[:, slot + 6:slot + 7],
                                     scale=ST[:, slot + 5:slot + 6]
                                     ).then_inc(s_nb, 1)
                    else:
                        s.activation(Z2B[:, u * D:(u + 1) * D],
                                     JROT_S[u % 3][:, :], AF.Identity,
                                     bias=ST[:, slot + 6:slot + 7],
                                     scale=ST[:, slot + 5:slot + 6]
                                     ).then_inc(s_z2, 1)
                if pend_nr:
                    kind, u = pend_nr.pop(0)
                    slot = SLOT[kind](u)
                    s.activation(ST[:, slot + 6:slot + 7],
                                 ST[:, slot + 4:slot + 5], AF.Identity,
                                 bias=0.0, scale=ST[:, slot + 5:slot + 6])
                    pend_ap.append((kind, u))

            for ses in fin_order:
                kind, idx = ses
                slot = SLOT[kind](idx)
                s.wait_ge(s_vs, ses_val[ses])
                s.activation(ST[:, slot + 5:slot + 6],
                             ST[:, slot + 3:slot + 4],
                             AF.Sqrt).then_inc(s_sq, 1)
                ln_flush()
                if kind == "L1" and idx < 8:
                    emit_cast(idx + 2)
                if kind != "L1":
                    pend_nr.append((kind, idx))
                if kind == "L1" and idx >= 3:
                    kq_evac(idx - 3)
                if ses == ("L1", T - 1):
                    for tt in (T - 3, T - 2, T - 1):
                        kq_evac(tt)
                    s.wait_ge(s_m, 1)
                    s.activation(MB[:, 0:512], MPS[:, 0:512], AF.Copy)
                    s.activation(MB[:, 512:1024], MPS[:, 512:1024],
                                 AF.Copy).then_inc(s_mev, 1)
                if ses == ("L3", 12):
                    sev(0)
                    sev(1)
                if ses == ("L3", 13):
                    relu_a(0, full=True)
                if ses == ("L3", 14):
                    relu_a(1, full=True)
                if ses == ("L3", 15):
                    sev(2)
            while pend_nr or pend_ap:
                ln_flush(pad=4)

            def gev(cc):
                s.wait_ge(s_gmm, cc + 1)
                s.activation(GT[0:64, cc * 512:(cc + 1) * 512],
                             KQ[cc % 2][0:64, :],
                             AF.Copy).then_inc(s_gev, 1)

            relu_a(2, full=True)
            relu_a(3, full=True)
            relu_a(4)
            relu_a(5)
            relu_a(6)
            relu_a(7)
            sev(3)
            gev(0)
            for c in range(8, 12):
                relu_a(c)
            gev(1)
            for c in range(12, 16):
                relu_a(c)
            gev(2)
            gev(3)

        @blk.tensor
        def _(te):
            te.wait_ge(s_wc, 2)
            te.wait_ge(s_idb, 16)

            def xt_t(j, t):
                return XT[:, j * 2048 + t * 128: j * 2048 + (t + 1) * 128]

            for t in range(T):
                te.wait_ge(s_trs[t % 2], 16 * (t // 2 + 1))
                if t >= 2:
                    te.wait_ge(s_ev, t - 1)
                kq = KQ[t % 2]
                for j in range(8):
                    te.matmul(kq[:, 0:128], lhsT=xt_t(j, t),
                              rhs=WKTB[:, j * 128:(j + 1) * 128],
                              start=(j == 0), stop=False)
                last = None
                for j in range(8):
                    last = te.matmul(kq[:, 128:256],
                                     lhsT=WQTB[:, j * 128:(j + 1) * 128],
                                     rhs=xt_t(j, t),
                                     start=False, stop=(j == 7))
                last.then_inc(s_kq, 1)
                te.wait_ge(s_ev, t + 1)
                for h in range(2):
                    last = te.matmul(
                        MPS[:, h * 512:(h + 1) * 512],
                        lhsT=KB[:, t * 128:(t + 1) * 128],
                        rhs=XNB[:, t * D + h * 512: t * D + (h + 1) * 512],
                        start=(t == 0), stop=(t == 15))
            last.then_inc(s_m, 1)

            def mm_s(cc):
                wv = tr_wait_vals(16 + 4 * cc + 3)
                te.wait_ge(s_trs[0], wv[0])
                te.wait_ge(s_trs[1], wv[1])
                if cc == 2:
                    te.wait_ge(s_sev, 1)
                elif cc == 3:
                    te.wait_ge(s_sev, 2)
                sp = KQ[cc % 2]
                for j in range(8):
                    te.matmul(sp[0:64, 0:512],
                              lhsT=WINB[:, j * 64:(j + 1) * 64],
                              rhs=XT[:, j * 2048 + cc * 512: j * 2048 + (cc + 1) * 512],
                              start=(j == 0), stop=False)
                te.matmul(sp[0:64, 0:512], lhsT=BINCB[:, :],
                          rhs=ONESB[:, :],
                          start=False, stop=True).then_inc(s_smm, 1)

            def warm(bank):
                te.matmul(bank[0:64, 0:64], lhsT=ONESB[:, 0:64],
                          rhs=ONESB[:, 0:64], start=True, stop=True)

            te.wait_ge(s_vs, ses_val[("L1", T - 1)])
            warm(JP1)
            te.wait_ge(s_mev, 1)
            te.wait_ge(s_wc, NW)
            te.wait_ge(s_ev, T)
            JROT = [MPS, JP0, JP1]
            for t in range(T):
                if t >= 5:
                    te.wait_ge(s_vs, ses_val[("L3", t - 5)])
                    warm(KQ0)
                if t >= 4:
                    te.wait_ge(s_vs, ses_val[("L2", t - 1)])
                    warm(KQ0)
                if t >= 3:
                    te.wait_ge(s_z2, t - 2)
                p = JROT[t % 3]
                for h in range(2):
                    te.matmul(p[:, h * 512:(h + 1) * 512],
                              lhsT=QTB[:, t * 128:(t + 1) * 128],
                              rhs=MB[:, h * 512:(h + 1) * 512],
                              start=True, stop=False)
                    last = te.matmul(
                        p[:, h * 512:(h + 1) * 512],
                        lhsT=IDB[:, :],
                        rhs=XNB[:, t * D + h * 512: t * D + (h + 1) * 512],
                        start=False, stop=True)
                last.then_inc(s_j, 1)
            mm_s(0)
            mm_s(1)

            def mm_h(c):
                te.wait_ge(s_sev, c // 4 + 1)
                if c == 0:
                    te.wait_ge(s_z2, 14)
                elif c == 1:
                    te.wait_ge(s_z2, 15)
                elif c == 2:
                    te.wait_ge(s_z2, T)
                else:
                    cp = c - 3
                    te.wait_ge(s_hev, cp + 1)
                    te.wait_ge(s_hbs, min(cp + 1, 4))
                    if cp >= 4:
                        te.wait_ge(s_hbv, cp - 3)
                hp = HROT[c % 3]
                last = None
                for j in range(8):
                    last = te.matmul(hp[:, j * 128:(j + 1) * 128],
                                     lhsT=BTAB[0:65, j * 128:(j + 1) * 128],
                                     rhs=SST[0:65, c * 128:(c + 1) * 128],
                                     start=True, stop=True)
                last.then_inc(s_hmm, 1)

            def mm_g(cc):
                te.wait_ge(s_hev, 4 * cc + 4)
                te.wait_ge(s_hbs, 4)
                if cc >= 1:
                    te.wait_ge(s_hbv, 4 * cc)
                if cc < 2:
                    te.wait_ge(s_sev, cc + 3)
                gv = KQ[cc % 2][0:64, :]
                for j in range(8):
                    te.matmul(gv, lhsT=CTAB[:, j * 64:(j + 1) * 64],
                              rhs=XNB[:, j * 2048 + cc * 512: j * 2048 + (cc + 1) * 512],
                              start=(j == 0), stop=False)
                te.matmul(gv, lhsT=B2CB[:, :],
                          rhs=ONESB[:, :],
                          start=False, stop=True).then_inc(s_gmm, 1)

            def mm_y(cc, k):
                t = 4 * cc + k
                kqy = t >= 8 and t % 2 == 1
                if t == 0:
                    te.wait_ge(s_hev, 15)
                    te.wait_ge(s_hbv, 11)
                te.wait_ge(s_gev, 4 if kqy else cc + 1)
                for h in range(2):
                    if t >= 10:
                        te.wait_ge(s_yv, 2 * (t - 2) + h + 1)
                    elif t == 8 or (1 <= t < 8):
                        te.wait_ge(s_yv, 2 * (t - 1) + h + 1)
                    bank = (KQ0[:, :] if h == 0 else KQ1[:, :]) if kqy \
                        else MPS[:, h * 512:(h + 1) * 512]
                    te.matmul(bank,
                              lhsT=GT[0:65, t * 128:(t + 1) * 128],
                              rhs=WOAB[0:65, h * 512:(h + 1) * 512],
                              start=True, stop=True).then_inc(s_y, 1)

            for c in range(0, 4):
                mm_h(c)
            mm_s(2)
            for c in range(4, 8):
                mm_h(c)
            mm_s(3)
            mm_h(8)
            mm_g(0)
            for c in range(9, 13):
                mm_h(c)
            mm_g(1)
            for c in range(13, 16):
                mm_h(c)
            for k in range(4):
                mm_y(0, k)
            mm_g(2)
            for k in range(4):
                mm_y(1, k)
            mm_g(3)
            for k in range(4):
                mm_y(2, k)
            for k in range(4):
                mm_y(3, k)

    return nc


def _layout_weights(inputs):
    wq = np.asarray(inputs["L0_attn_W_Q"], np.float32)
    wk = np.asarray(inputs["L0_attn_W_K"], np.float32)
    gam = np.asarray(inputs["L0_attn_gamma"], np.float32)
    bet = np.asarray(inputs["L0_attn_beta"], np.float32)
    win = np.asarray(inputs["L0_ffn_Win"], np.float32)
    binv = np.asarray(inputs["L0_ffn_bin_"], np.float32)
    w1 = np.asarray(inputs["L0_ffn_W1"], np.float32)
    b1 = np.asarray(inputs["L0_ffn_b1"], np.float32)
    w2 = np.asarray(inputs["L0_ffn_W2"], np.float32)
    b2 = np.asarray(inputs["L0_ffn_b2"], np.float32)
    wout = np.asarray(inputs["L0_ffn_Wout"], np.float32)
    bout = np.asarray(inputs["L0_ffn_bout"], np.float32)

    def tposed(w):
        return np.ascontiguousarray(
            w.T.reshape(8, 128, 128).transpose(1, 0, 2).reshape(128, 1024))

    bta = np.zeros((65, 8, 128), np.float32)
    cta = np.zeros((128, 8, 64), np.float32)
    for j in range(8):
        for cix in range(128):
            m = 8 * j + cix // 16
            i = cix % 16
            bta[m, j, cix] = w1[m, i]
            bta[64, j, cix] = b1[m, i]
            cta[cix, j, m] = w2[m, i]

    return {
        "wqt": tposed(wq),
        "wkt": tposed(wk),
        "gam": np.ascontiguousarray(np.broadcast_to(gam, (128, 1024))),
        "bet": np.ascontiguousarray(np.broadcast_to(bet, (128, 1024))),
        "win": np.ascontiguousarray(
            win.reshape(8, 128, 64).transpose(1, 0, 2).reshape(128, 512)),
        "bta": np.ascontiguousarray(bta.reshape(65, 1024)),
        "cta": np.ascontiguousarray(cta.reshape(128, 512)),
        "woa": np.ascontiguousarray(np.concatenate([wout, bout[None, :]], 0)),
        "b2c": np.ascontiguousarray(b2[None, :]),
        "binc": np.ascontiguousarray(binv[None, :]),
        "idb": np.eye(128, dtype=ml_dtypes.bfloat16),
    }


def kernel(**inputs):
    x_full = np.asarray(inputs["L0_attn_J"], np.float32)
    wmaps = _layout_weights(inputs)
    nc = build_nc()
    in_maps = []
    for c in range(N_CORES):
        m = dict(wmaps)
        m["x"] = np.ascontiguousarray(x_full[c])
        in_maps.append(m)
    res = run_bass_kernel_spmd(nc, in_maps, core_ids=list(range(N_CORES)))
    return np.stack([np.asarray(r["y"], np.float32) for r in res.results], 0)
